# version 03 (final)
# baseline (speedup 1.0000x reference)
import numpy as np
import ml_dtypes
import concourse.bass as bass
import concourse.mybir as mybir
from concourse.bass_utils import run_bass_kernel_spmd

F32 = mybir.dt.float32
BF16 = mybir.dt.bfloat16
I8 = mybir.dt.int8
AF = mybir.ActivationFunctionType
ALU = mybir.AluOpType
AX = mybir.AxisListType

D = 2048
T = 4096
TH = 2048
DT = 16
HD = 128
INTER = 8192
NEG = -1e9
SCALE = float(np.float32(1.0) / np.sqrt(np.float32(HD)))
LN_EPS = 1e-5
RMS_EPS = 1e-6
PAIRS = [[0, 1], [2, 3], [4, 5], [6, 7]]
QC_ORDER = [0, 4, 1, 5, 2, 6, 3, 7]

bf16 = ml_dtypes.bfloat16


def _bf(x):
    return np.ascontiguousarray(np.asarray(x, dtype=np.float32).astype(bf16))


def _f(x):
    return np.ascontiguousarray(np.asarray(x, dtype=np.float32))


def tile_rows(w):
    n = w.shape[0] // 128
    return np.ascontiguousarray(
        w.reshape(n, 128, w.shape[1]).transpose(1, 0, 2).reshape(128, -1))


def host_constants():
    c = {}
    c["ident"] = _bf(np.eye(128))
    M = np.zeros((128, 128), dtype=np.float32)
    for r in range(32):
        M[r + 32, r] = 1.0
        M[r, r + 32] = 1.0
    c["mperm"] = _bf(M)
    q = np.arange(128)
    c["triu"] = _bf(np.where(q[None, :] > q[:, None], NEG, 0.0))
    c["tril"] = _bf(np.where(q[None, :] < q[:, None], NEG, 0.0))
    st = np.zeros((128, 16 * 128), dtype=np.float32)
    cc = np.arange(128)
    for o in range(16):
        st[:, o * 128:(o + 1) * 128] = np.where(
            (16 * cc[None, :] + 15) > (o * 128 + q[:, None]), NEG, 0.0)
    c["stair"] = _bf(st)
    c["onescol"] = _f(np.ones((128, 1)))
    c["onescol_b"] = _bf(np.ones((128, 1)))
    c["onesrow"] = _f(np.ones((1, 128)))
    c["halfrow"] = _f(0.5 * np.ones((1, 128)))
    inv = (1.0 / (10000.0 ** (np.arange(32, dtype=np.float32) * np.float32(2.0 / 64.0)))).astype(np.float32)

    def ropetab(pos):
        ang = pos.astype(np.float32)[None, :] * inv[:, None]
        cos = np.cos(ang).astype(np.float32)
        sin = np.sin(ang).astype(np.float32)
        Ct = np.ones((128, pos.shape[0]), dtype=np.float32)
        St = np.zeros((128, pos.shape[0]), dtype=np.float32)
        Ct[0:32] = cos
        Ct[32:64] = cos
        St[0:32] = -sin
        St[32:64] = sin
        return _f(Ct), _f(St)

    c["ct"], c["st_t"] = ropetab(np.arange(T))
    c["ctc"], c["stc"] = ropetab(np.arange(256) * 16 + 15)
    return c


def build_nc():
    nc = bass.Bass("TRN2", target_bir_lowering=False, debug=False)

    def di(name, shape, dt=F32):
        return nc.dram_tensor(name, list(shape), dt, kind="ExternalInput").ap()

    A = {}
    A["x0T"] = di("x0T", [D, T])
    A["x0ownT"] = di("x0ownT", [D, TH])
    A["wqkv"] = di("wqkv", [D, 3072])
    A["bqkv"] = di("bqkv", [128, 16])
    A["vbias"] = di("vbias", [1, 1024])
    A["wproj"] = di("wproj", [128, 8 * D])
    A["bproj"] = di("bproj", [128, 16])
    A["wqlk"] = di("wqlk", [D, 3072])
    A["bqlk"] = di("bqlk", [128, 16])
    A["blv"] = di("blv", [1, 1024])
    A["wo"] = di("wo", [128, 8 * D])
    A["bo"] = di("bo", [128, 16])
    A["wc"] = di("wc", [128, 16])
    A["wk"] = di("wk", [128, DT * 128])
    A["wv"] = di("wv", [128, DT * 128])
    A["bk"] = di("bk", [128, 1])
    A["bv"] = di("bv", [1, 128])
    A["qnw"] = di("qnw", [128, 1])
    A["knw"] = di("knw", [128, 1])
    A["sinkk"] = di("sinkk", [128, 8])
    A["sinkv"] = di("sinkv", [1, 1024])
    A["wg8"] = di("wg8", [D, INTER], I8)
    A["wu8"] = di("wu8", [D, INTER], I8)
    A["wd8"] = di("wd8", [INTER, D], I8)
    A["sg"] = di("sg", [128, 64])
    A["su"] = di("su", [128, 64])
    A["sd"] = di("sd", [128, 16])
    A["epsr"] = di("epsr", [1, 2])
    A["bcr"] = di("bcr", [1, 1])
    for nm, sh, dt in [("ident", [128, 128], BF16), ("mperm", [128, 128], BF16),
                       ("triu", [128, 128], BF16), ("tril", [128, 128], BF16),
                       ("stair", [128, 2048], BF16), ("onescol", [128, 1], F32),
                       ("onescol_b", [128, 1], BF16), ("onesrow", [1, 128], F32),
                       ("halfrow", [1, 128], F32),
                       ("ct", [128, T], F32), ("st_t", [128, T], F32),
                       ("ctc", [128, 256], F32), ("stc", [128, 256], F32)]:
        A[nm] = di(nm, sh, dt)

    out_x = nc.dram_tensor("out_x", [D, TH], F32, kind="ExternalOutput").ap()
    import os
    DBG = bool(int(os.environ.get("BASSDBG", "0")))

    def dint(name, shape, dt):
        return nc.dram_tensor(name, list(shape), dt, kind="Internal").ap()

    sQT = dint("sQT", [1024, T], BF16)
    sKT = dint("sKT", [1024, T], BF16)
    sV = dint("sV", [T, 1024], BF16)
    sQbT = dint("sQbT", [1024, T], BF16)
    sLkT = dint("sLkT", [1024, T], BF16)
    sLv = dint("sLv", [T, 1024], BF16)
    wgb = dint("wgb", [64, DT, 128, 128], BF16)
    wub = dint("wub", [64, DT, 128, 128], BF16)
    wdb = dint("wdb", [16, 64, 128, 128], BF16)
    x2c = [dint(f"x2c{j}", [D, 512], F32) for j in range(4)]
    agx2 = [dint(f"agx2{j}", [2 * D, 512], F32) for j in range(4)]
    rs1in = [dint(f"rs1in{j}", [2 * D, 512], F32) for j in range(4)]
    rs1out = [dint(f"rs1out{j}", [D, 512], F32) for j in range(4)]
    rs2in = [dint(f"rs2in{j}", [2 * D, 512], F32) for j in range(4)]
    rs2out = [dint(f"rs2out{j}", [D, 512], F32) for j in range(4)]

    from contextlib import ExitStack
    ctx = ExitStack()

    def S(name, shape, dt):
        return ctx.enter_context(nc.sbuf_tensor(name, list(shape), dt))

    BIGB = S("BIGB", [128, 49152], BF16)
    XLNB = S("XLNB", [128, 16384], BF16)
    F32B = S("F32B", [128, 9216], F32)
    CONV8 = S("CONV8", [128, 4096], I8)
    ENTR = S("ENTR", [128, 4096], BF16)
    STG = S("STG", [128, 1024], F32)
    CKTAB = S("CKTAB", [128, 256], F32)
    SKTAB = S("SKTAB", [128, 256], F32)
    CKT = S("CKT", [128, 256], BF16)
    CVT = S("CVT", [128, 256], BF16)
    IDENT = S("IDENT", [128, 128], BF16)
    MPERM = S("MPERM", [128, 128], BF16)
    TRIU = S("TRIU", [128, 128], BF16)
    TRIL = S("TRIL", [128, 128], BF16)
    STAIR = S("STAIR", [128, 2048], BF16)
    ONESC = S("ONESC", [128, 1], F32)
    ONESCB = S("ONESCB", [128, 1], BF16)
    ONESR = S("ONESR", [1, 128], F32)
    HALFR = S("HALFR", [1, 128], F32)
    BQKV = S("BQKV", [128, 16], F32)
    VBIAS = S("VBIAS", [1, 1024], F32)
    BPROJ = S("BPROJ", [128, 16], F32)
    BQLK = S("BQLK", [128, 16], F32)
    BLV = S("BLV", [1, 1024], F32)
    BO = S("BO", [128, 16], F32)
    WCB = S("WCB", [128, 16], BF16)
    BKR = S("BKR", [128, 1], F32)
    BVR = S("BVR", [1, 128], F32)
    QNW = S("QNW", [128, 1], F32)
    KNW = S("KNW", [128, 1], F32)
    SINKK = S("SINKK", [128, 8], BF16)
    SINKV = S("SINKV", [1, 1024], BF16)
    SGB = S("SGB", [128, 64], F32)
    SUB = S("SUB", [128, 64], F32)
    SDB = S("SDB", [128, 16], F32)
    EPSR = S("EPSR", [1, 2], F32)
    BCR = S("BCR", [1, 1], F32)
    ROWS = S("ROWS", [1, 1536], F32)
    PS = [ctx.enter_context(nc.psum_tensor(f"ps{i}", [128, 512], F32)) for i in range(8)]

    sem_names = (["ld", "ld2", "st", "st2", "m", "v", "c",
                  "cst", "cv0", "cv1", "cs0", "cs1", "lw0", "lw1", "xcs", "tbs",
                  "fx0", "fx1", "wg0", "wg1", "wd0", "wd1", "fl0", "fl1", "wps", "wkv"]
                 + [f"sq{i}" for i in range(4)] + [f"sv{i}" for i in range(4)]
                 + ["sg0", "sg1", "sr0", "sr1"]
                 + [f"cc{i}" for i in range(12)])
    SEM = {n: ctx.enter_context(nc.semaphore(n)) for n in sem_names}

    ENGS = ("sync", "tensor", "vector", "scalar", "gpsimd")

    class Plan:
        def __init__(self):
            self.items = {e: [] for e in ENGS}
            self.cnt = {n: 0 for n in sem_names}

        def op(self, eng, build, incs=()):
            incs = list(incs)
            for s_, a_ in incs:
                self.cnt[s_] += a_
            def run(e, _b=build, _i=incs):
                inst = _b(e)
                for s_, a_ in _i:
                    inst = inst.then_inc(SEM[s_], a_)
            self.items[eng].append(run)

        def wait(self, eng, sem, val=None):
            v = self.cnt[sem] if val is None else val
            if v <= 0:
                return
            self.items[eng].append(lambda e, _s=sem, _v=v: e.wait_ge(SEM[_s], _v))

        def val(self, sem):
            return self.cnt[sem]

    P = Plan()

    SLOT = {}

    def slot_wait(eng, key):
        if key in SLOT:
            s_, v_ = SLOT[key]
            P.wait(eng, s_, v_)

    def slot_set(key, sem):
        SLOT[key] = (sem, P.cnt[sem])

    def dma(eng, sem, out, in_):
        P.op(eng, lambda e, _o=out, _i=in_: e.dma_start(out=_o, in_=_i), [(sem, 16)])

    STORE_SEMS = ["st", "st2", "cs0", "cs1"] + [f"sq{i}" for i in range(4)] \
        + [f"sv{i}" for i in range(4)] + ["sg0", "sg1", "sr0", "sr1"]

    def wait_stores(eng):
        for s_ in STORE_SEMS:
            P.wait(eng, s_)

    for nm, dst in [("ident", IDENT), ("mperm", MPERM), ("triu", TRIU),
                    ("tril", TRIL), ("stair", STAIR), ("onescol", ONESC),
                    ("onescol_b", ONESCB), ("onesrow", ONESR), ("halfrow", HALFR),
                    ("bqkv", BQKV), ("vbias", VBIAS), ("bproj", BPROJ),
                    ("bqlk", BQLK), ("blv", BLV), ("bo", BO), ("bk", BKR),
                    ("bv", BVR), ("qnw", QNW), ("knw", KNW), ("sg", SGB),
                    ("su", SUB), ("sd", SDB), ("epsr", EPSR), ("bcr", BCR),
                    ("ctc", CKTAB), ("stc", SKTAB)]:
        dma("sync", "cst", dst[:, :], A[nm])
    dma("sync", "cst", F32B[:, 0:16], A["wc"])
    dma("sync", "cst", F32B[:, 16:24], A["sinkk"])
    dma("sync", "cst", F32B[0:1, 1024:2048], A["sinkv"])
    P.wait("vector", "cst")
    P.op("vector", lambda e: e.tensor_copy(WCB[:, :], F32B[:, 0:16]), [("v", 1)])
    P.op("vector", lambda e: e.tensor_copy(SINKK[:, :], F32B[:, 16:24]), [("v", 1)])
    P.op("vector", lambda e: e.tensor_copy(SINKV[0:1, :], F32B[0:1, 1024:2048]), [("v", 1)])

    conv_jobs = ([("g", i) for i in range(DT)] + [("u", i) for i in range(DT)]
                 + [("d", i) for i in range(64)])
    cv_done = {}
    st_done = {}
    for j, (kind, idx) in enumerate(conv_jobs):
        src = {"g": A["wg8"], "u": A["wu8"], "d": A["wd8"]}[kind]
        src = src.rearrange("(n p) c -> n p c", p=128)[idx]
        ncols = INTER if kind in ("g", "u") else D
        nld = ncols // 2048
        if j >= 1:
            P.wait("gpsimd", "v", cv_done[j - 1][0])
            P.wait("gpsimd", "c", cv_done[j - 1][1])
        xoff = (j % 2) * 8192
        for li in range(nld):
            half = (j * 4 + li) % 2
            csem = f"cv{half}"
            if j * 4 + li >= 2:
                P.wait("sync", "v")
                P.wait("sync", "c")
            dma("sync", csem, CONV8[:, half * 2048: half * 2048 + 2048],
                src[:, li * 2048:(li + 1) * 2048])
            P.wait("vector", csem)
            P.wait("scalar", csem)
            if j >= 2:
                ssem, sval = st_done.get(j - 2, ("st", 0))
                P.wait("vector", ssem, sval)
                P.wait("scalar", ssem, sval)
            P.op("vector", lambda e, _h=half, _li=li, _x=xoff: e.tensor_copy(
                XLNB[:, _x + _li * 2048: _x + _li * 2048 + 1024],
                CONV8[:, _h * 2048: _h * 2048 + 1024]), [("v", 1)])
            P.op("scalar", lambda e, _h=half, _li=li, _x=xoff: e.activation(
                XLNB[:, _x + _li * 2048 + 1024: _x + _li * 2048 + 2048],
                CONV8[:, _h * 2048 + 1024: _h * 2048 + 2048], AF.Copy), [("c", 1)])
        cv_done[j] = (P.val("v"), P.val("c"))
        P.wait("gpsimd", "v")
        P.wait("gpsimd", "c")
        nt = 64 if kind in ("g", "u") else 16
        dstw = {"g": wgb, "u": wub, "d": wdb}[kind]
        ssem = f"cs{j % 2}"
        dma("gpsimd", ssem, dstw[:, idx].rearrange("o p c -> p o c"),
            XLNB.ap().rearrange("p (o c) -> p o c", c=128)[:, xoff // 128: xoff // 128 + nt, :])
        st_done[j] = (ssem, P.val(ssem))


    def load_wres(src_ap, dst_off=0, width=3072):
        last_v = {}
        P.wait("sync", "m")
        P.wait("sync", "v")
        P.wait("sync", "c")
        P.wait("vector", "m")
        for dtile in range(DT):
            half = dtile % 2
            if dtile >= 2:
                P.wait("sync", "v", last_v[dtile - 2])
            dma("sync", f"lw{half}", F32B[:, half * 3072: half * 3072 + width],
                src_ap.rearrange("(n p) c -> n p c", p=128)[dtile])
            P.wait("vector", f"lw{half}")
            for k in range((width + 1023) // 1024):
                w = min(1024, width - k * 1024)
                P.op("vector", lambda e, _d=dtile, _h=half, _k=k, _w=w: e.tensor_copy(
                    BIGB[:, dst_off + _d * width + _k * 1024: dst_off + _d * width + _k * 1024 + _w],
                    F32B[:, _h * 3072 + _k * 1024: _h * 3072 + _k * 1024 + _w]), [("v", 1)])
            last_v[dtile] = P.val("v")

    def ln_chunk():
        SUM, SUMSQ, MB, RB = PS[4], PS[5], PS[6], PS[7]
        sqs = [F32B[:, 8192:8704], F32B[:, 8704:9216]]
        P.wait("tensor", "v")
        P.wait("scalar", "v")
        sq_used = {}
        for dtile in range(DT):
            if dtile >= 2:
                P.wait("scalar", "m", sq_used[dtile - 2])
            P.op("scalar", lambda e, _d=dtile: e.activation(
                sqs[_d % 2], F32B[:, _d * 512:(_d + 1) * 512], AF.Square), [("c", 1)])
            P.wait("tensor", "c")
            P.op("tensor", lambda e, _d=dtile: e.matmul(
                SUM[0:1, :], ONESC[:, 0:1], F32B[:, _d * 512:(_d + 1) * 512],
                start=(_d == 0), stop=(_d == DT - 1)))
            P.op("tensor", lambda e, _d=dtile: e.matmul(
                SUMSQ[0:1, :], ONESC[:, 0:1], sqs[_d % 2],
                start=(_d == 0), stop=(_d == DT - 1)), [("m", 1)])
            sq_used[dtile] = P.val("m")
        mrow = ROWS[0:1, 0:512]
        msq = ROWS[0:1, 512:1024]
        var = ROWS[0:1, 1024:1536]
        lnv = ROWS[0:1, 512:1024]
        rrow = ROWS[0:1, 1024:1536]
        P.wait("vector", "m")
        P.op("vector", lambda e: e.tensor_scalar_mul(mrow, SUM[0:1, :], 1.0 / D), [("v", 1)])
        P.wait("vector", "v")
        P.op("vector", lambda e: e.tensor_mul(msq, mrow, mrow), [("v", 1)])
        P.wait("vector", "v")
        P.op("vector", lambda e: e.scalar_tensor_tensor(
            var, SUMSQ[0:1, :], 1.0 / D, msq, ALU.mult, ALU.subtract), [("v", 1)])
        P.wait("scalar", "v")
        P.op("scalar", lambda e: e.activation(lnv, var, AF.Ln, bias=EPSR[0:1, 0:1]), [("c", 1)])
        P.wait("scalar", "c")
        P.op("scalar", lambda e: e.activation(rrow, lnv, AF.Exp, scale=-0.5), [("c", 1)])
        P.wait("tensor", "v")
        P.op("tensor", lambda e: e.matmul(MB[:, :], ONESR[0:1, :], mrow, start=True, stop=True))
        P.wait("tensor", "c")
        P.op("tensor", lambda e: e.matmul(RB[:, :], ONESR[0:1, :], rrow, start=True, stop=True), [("m", 1)])
        P.wait("vector", "m")
        t1 = F32B[:, 8192:8704]
        for dtile in range(DT):
            P.op("vector", lambda e, _d=dtile: e.tensor_tensor(
                t1, F32B[:, _d * 512:(_d + 1) * 512], MB[:, :], ALU.subtract), [("v", 1)])
            P.wait("vector", "v")
            P.op("vector", lambda e, _d=dtile: e.tensor_tensor(
                XLNB[:, _d * 512:(_d + 1) * 512], t1, RB[:, :], ALU.mult), [("v", 1)])
            P.wait("vector", "v")

    def proj16(woff, wwidth, ot, bank, rhs_n=512):
        for dtile in range(DT):
            P.op("tensor", lambda e, _d=dtile: e.matmul(
                bank[:, 0:rhs_n],
                BIGB[:, woff + _d * wwidth + ot * 128: woff + _d * wwidth + ot * 128 + 128],
                XLNB[:, _d * 512: _d * 512 + rhs_n],
                start=(_d == 0), stop=(_d == DT - 1)),
                [("m", 1)] if dtile == DT - 1 else ())

    load_wres(A["wqkv"], 0, 3072)
    P.wait("vector", "cs0")
    P.wait("vector", "cs1")
    P.wait("scalar", "cs0")
    P.wait("scalar", "cs1")
    x0r = A["x0T"].rearrange("(n p) t -> p n t", p=128)
    for tc in range(8):
        P.wait("sync", "v")
        dma("sync", "xcs", F32B.ap().rearrange("p (n c) -> p n c", c=512)[:, 0:16, :],
            x0r[:, :, tc * 512:(tc + 1) * 512])
        P.wait("scalar", "xcs")
        P.wait("tensor", "xcs")
        P.wait("vector", "xcs")
        ln_chunk()
        P.wait("tensor", "v")
        ev_done = {}
        for ot in range(16):
            bank = PS[ot % 2]
            if ot >= 2:
                P.wait("tensor", "c", ev_done[ot - 2])
            proj16(0, 3072, ot, bank)
            stg = XLNB[:, 8192 + (ot % 4) * 512: 8192 + (ot % 4) * 512 + 512]
            slot_wait("scalar", ("qkstg", ot % 4))
            P.wait("scalar", "m")
            P.op("scalar", lambda e, _o=ot, _b=bank, _s=stg: e.activation(
                _s, _b[:, :], AF.Identity, bias=BQKV[:, _o:_o + 1]), [("c", 1)])
            ev_done[ot] = P.val("c")
            P.wait("gpsimd", "c")
            dst = sQT if ot < 8 else sKT
            oo = ot % 8
            dma("gpsimd", f"sq{ot % 4}", dst.rearrange("(n p) t -> n p t", p=128)[oo][:, tc * 512:(tc + 1) * 512], stg)
            slot_set(("qkstg", ot % 4), f"sq{ot % 4}")
        for vt in range(8):
            tt, nb = vt // 2, vt % 2
            bank = PS[2 + vt % 2]
            if vt >= 2:
                P.wait("tensor", "c", ev_done[("v", vt - 2)])
            for dtile in range(DT):
                P.op("tensor", lambda e, _d=dtile, _t=tt, _n=nb, _b=bank: e.matmul(
                    _b[:, :],
                    XLNB[:, _d * 512 + _t * 128: _d * 512 + _t * 128 + 128],
                    BIGB[:, _d * 3072 + 2048 + _n * 512: _d * 3072 + 2048 + _n * 512 + 512],
                    start=(_d == 0), stop=False))
            P.op("tensor", lambda e, _n=nb, _b=bank: e.matmul(
                _b[:, :], ONESR[0:1, :], VBIAS[0:1, _n * 512:(_n + 1) * 512],
                start=False, stop=True), [("m", 1)])
            stg = XLNB[:, 8192 + ((vt % 4)) * 512 + 2048: 8192 + (vt % 4) * 512 + 2048 + 512]
            slot_wait("scalar", ("vstg", vt % 4))
            P.wait("scalar", "m")
            P.op("scalar", lambda e, _b=bank, _s=stg: e.activation(_s, _b[:, :], AF.Copy), [("c", 1)])
            ev_done[("v", vt)] = P.val("c")
            P.wait("gpsimd", "c")
            dma("gpsimd", f"sv{vt % 4}", sV[tc * 512 + tt * 128: tc * 512 + tt * 128 + 128,
                                   nb * 512:(nb + 1) * 512], stg)
            slot_set(("vstg", vt % 4), f"sv{vt % 4}")

    wprj_off = 26624
    P.wait("sync", "m")
    P.wait("sync", "v")
    P.wait("sync", "c")
    P.wait("vector", "m")
    for k in range(4):
        dma("sync", "wps", F32B[:, 0:4096], A["wproj"][:, k * 4096:(k + 1) * 4096])
        P.wait("vector", "wps")
        for u in range(4):
            P.op("vector", lambda e, _k=k, _u=u: e.tensor_copy(
                BIGB[:, wprj_off + _k * 4096 + _u * 1024: wprj_off + _k * 4096 + (_u + 1) * 1024],
                F32B[:, _u * 1024:(_u + 1) * 1024]), [("v", 1)])
        P.wait("sync", "v")

    KH = [0, 4096]
    VH = [8192, 12288]
    QH = [16384, 16896]
    EXPO = 17408
    CTXQ = 21504
    RBB = 25600

    def flash_a():
        wait_stores("sync")
        P.wait("sync", "m")
        it = -1
        last_mm = {}
        ld_done = {}
        ctx_tt = {}
        den_rd = {}
        qkT = sQT.rearrange("(n p) t -> n p t", p=128)
        kkT = sKT.rearrange("(n p) t -> n p t", p=128)
        vv = sV.rearrange("t (n c) -> ", ) if False else None
        proj_done = {}
        for qi, qc in enumerate(QC_ORDER):
            ktn = 4 * qc + 4
            for h in range(8):
                it += 1
                sl = it % 2
                if it >= 2:
                    P.wait("sync", "m", last_mm[it - 2])
                fsem = f"fl{sl}"
                dma("sync", fsem, BIGB[:, KH[sl]: KH[sl] + ktn * 128],
                    kkT[h][:, 0:ktn * 128])
                dma("sync", fsem,
                    BIGB.ap().rearrange("p (n c) -> p n c", c=128)[:, VH[sl] // 128: VH[sl] // 128 + ktn, :],
                    sV.rearrange("(n p) c -> p n c", p=128)[:, 0:ktn, h * 128:(h + 1) * 128])
                dma("sync", fsem, BIGB[:, QH[sl]: QH[sl] + 512],
                    qkT[h][:, qc * 512:(qc + 1) * 512])
                ld_tgt = P.val(fsem)
                AVP, DEN, RBP = PS[2], PS[3], PS[4]
                P.wait("tensor", fsem, ld_tgt)
                if it >= 1:
                    P.wait("tensor", "v", ctx_tt.get(it - 1, 0))
                    P.wait("tensor", "c", den_rd.get(it - 1, 0))
                exp_done = {}
                av_done = {}
                sgrp = {}
                for kt in range(ktn):
                    diag = kt >= 4 * qc
                    j0 = kt - 4 * qc if diag else 0
                    c0 = j0 * 128
                    ncol = 512 - c0
                    bank = PS[kt % 2]
                    if kt >= 2:
                        P.wait("tensor", "c", exp_done[kt - 2])
                    P.op("tensor", lambda e, _b=bank, _sl=sl, _kt=kt, _c0=c0, _n=ncol: e.matmul(
                        _b[:, _c0:_c0 + _n],
                        BIGB[:, KH[_sl] + _kt * 128: KH[_sl] + _kt * 128 + 128],
                        BIGB[:, QH[_sl] + _c0: QH[_sl] + _c0 + _n],
                        start=True, stop=(not diag)),
                        () if diag else [("m", 1)])
                    if diag:
                        P.op("tensor", lambda e, _b=bank, _c0=c0: e.matmul(
                            _b[:, _c0:_c0 + 128], TRIU[:, :], IDENT[:, :],
                            start=False, stop=True), [("m", 1)])
                    sgrp[kt] = P.val("m")
                    ex = EXPO + (kt % 8) * 512
                    P.wait("scalar", "m", sgrp[kt])
                    if kt >= 8:
                        P.wait("scalar", "m", av_done[kt - 8])
                    P.op("scalar", lambda e, _b=bank, _c0=c0, _n=ncol, _x=ex: e.activation(
                        BIGB[:, _x: _x + _n], _b[:, _c0:_c0 + _n], AF.Exp, scale=SCALE), [("c", 1)])
                    exp_done[kt] = P.val("c")
                    P.wait("tensor", "c", exp_done[kt])
                    P.op("tensor", lambda e, _c0=c0, _n=ncol, _x=ex, _kt=kt, _ktn=ktn: e.matmul(
                        DEN[0:1, _c0:_c0 + _n], ONESCB[:, 0:1], BIGB[:, _x: _x + _n],
                        start=(_kt == 0), stop=(_kt == _ktn - 1)))
                    P.op("tensor", lambda e, _c0=c0, _n=ncol, _x=ex, _kt=kt, _sl=sl, _ktn=ktn: e.matmul(
                        AVP[:, _c0:_c0 + _n],
                        BIGB[:, VH[_sl] + _kt * 128: VH[_sl] + _kt * 128 + 128],
                        BIGB[:, _x: _x + _n],
                        start=(_kt == 0), stop=(_kt == _ktn - 1)), [("m", 1)])
                    av_done[kt] = P.val("m")
                last_mm[it] = P.val("m")
                denr = ROWS[0:1, 0:512]
                lnr = ROWS[0:1, 512:1024]
                recr = ROWS[0:1, 1024:1536]
                P.wait("scalar", "m", last_mm[it])
                P.op("scalar", lambda e: e.activation(denr, DEN[0:1, :], AF.Copy), [("c", 1)])
                P.wait("scalar", "c")
                P.op("scalar", lambda e: e.activation(lnr, denr, AF.Ln), [("c", 1)])
                P.wait("scalar", "c")
                P.op("scalar", lambda e: e.activation(recr, lnr, AF.Exp, scale=-1.0), [("c", 1)])
                den_rd[it] = P.val("c")
                P.wait("tensor", "c", den_rd[it])
                P.op("tensor", lambda e: e.matmul(RBP[:, :], ONESR[0:1, :], recr,
                                                  start=True, stop=True), [("m", 1)])
                rb = RBB + (it % 2) * 512
                P.wait("vector", "m")
                if qi > 0:
                    P.wait("vector", "m", proj_done.get(qi - 1, 0))
                P.op("vector", lambda e, _rb=rb: e.tensor_copy(
                    BIGB[:, _rb:_rb + 512], RBP[:, :]), [("v", 1)])
                P.wait("vector", "v")
                P.op("vector", lambda e, _h=h, _rb=rb: e.tensor_tensor(
                    BIGB[:, CTXQ + _h * 512: CTXQ + _h * 512 + 512],
                    AVP[:, :], BIGB[:, _rb:_rb + 512], ALU.mult), [("v", 1)])
                ctx_tt[it] = P.val("v")
            P.wait("tensor", "v", ctx_tt[it])
            pj_ev = {}
            for o2 in range(16):
                bank = PS[5 + o2 % 2]
                if o2 >= 2:
                    P.wait("tensor", "v", pj_ev[o2 - 2])
                for dh in range(8):
                    P.op("tensor", lambda e, _d=dh, _o=o2, _b=bank: e.matmul(
                        _b[:, :],
                        BIGB[:, wprj_off + _d * 2048 + _o * 128: wprj_off + _d * 2048 + _o * 128 + 128],
                        BIGB[:, CTXQ + _d * 512: CTXQ + _d * 512 + 512],
                        start=(_d == 0), stop=(_d == 7)),
                        [("m", 1)] if dh == 7 else ())
                stg = STG[:, (o2 % 2) * 512:(o2 % 2) * 512 + 512]
                slot_wait("vector", ("stg", o2 % 2))
                P.wait("vector", "m")
                P.op("vector", lambda e, _s=stg, _b=bank: e.tensor_copy(_s, _b[:, :]), [("v", 1)])
                pj_ev[o2] = P.val("v")
                P.wait("gpsimd", "v")
                dma("gpsimd", f"sg{o2 % 2}", rs1in[qc % 4][(qc // 4) * 2048 + o2 * 128:
                                                   (qc // 4) * 2048 + o2 * 128 + 128, :], stg)
                slot_set(("stg", o2 % 2), f"sg{o2 % 2}")
            proj_done[qi] = P.val("m")
            if qc >= 4:
                cci = qc - 4
                P.wait("gpsimd", "sg0")
                P.wait("gpsimd", "sg1")
                P.op("gpsimd", lambda e, _c=cci: e.collective_compute(
                    "ReduceScatter", ALU.add, replica_groups=PAIRS,
                    ins=[rs1in[_c].opt()], outs=[rs1out[_c].opt()]), [(f"cc{cci}", 1)])

    flash_a()

    def ffn(app):
        rsout = rs1out if app == 0 else rs2out
        basebias = BPROJ if app == 0 else BO
        ccbase = 0 if app == 0 else 8
        fuse_done = {}
        for fc in range(4):
            P.wait("sync", f"cc{ccbase + fc}", 1)
            P.wait("sync", "v")
            P.wait("sync", "m")
            dma("sync", "xcs", F32B.ap().rearrange("p (n c) -> p n c", c=512)[:, 0:16, :],
                rsout[fc].rearrange("(n p) t -> p n t", p=128))
            xsrc = (A["x0ownT"].rearrange("(n p) t -> p n t", p=128) if app == 0
                    else x2c[fc].rearrange("(n p) t -> p n t", p=128))
            for dtile in range(DT):
                half = dtile % 2
                sq = F32B[:, 8192 + half * 512: 8192 + half * 512 + 512]
                if dtile >= 2:
                    P.wait("sync", "v")
                if app == 0:
                    dma("sync", f"fx{half}", sq, xsrc[:, dtile, fc * 512:(fc + 1) * 512])
                else:
                    dma("sync", f"fx{half}", sq, xsrc[:, dtile, :])
                P.wait("vector", f"fx{half}")
                P.wait("vector", "xcs")
                P.op("vector", lambda e, _d=dtile, _s=sq: e.scalar_tensor_tensor(
                    F32B[:, _d * 512:(_d + 1) * 512], _s, basebias[:, _d:_d + 1],
                    F32B[:, _d * 512:(_d + 1) * 512], ALU.add, ALU.add), [("v", 1)])
            ln_chunk()
            P.wait("tensor", "v")
            wg_ld = {}
            for ot in range(64):
                slg = (ot % 2) * 2
                slu = slg + 1
                wsem = f"wg{ot % 2}"
                if ot >= 2:
                    P.wait("sync", "m", wg_ld[ot - 2])
                dma("sync", wsem, BIGB.ap().rearrange("p (n c) -> p n c", c=128)[:, 256 + slg * 16:256 + slg * 16 + 16, :],
                    wgb.rearrange("o n p c -> o p n c")[ot])
                dma("sync", wsem, BIGB.ap().rearrange("p (n c) -> p n c", c=128)[:, 256 + slu * 16:256 + slu * 16 + 16, :],
                    wub.rearrange("o n p c -> o p n c")[ot])
                ldt = P.val(wsem)
                P.wait("tensor", wsem, ldt)
                if ot >= 1:
                    P.wait("tensor", "c", silu_dn[ot - 1])
                    P.wait("tensor", "v", fuse_done.get((fc, ot - 1), 0))
                for dtile in range(DT):
                    P.op("tensor", lambda e, _d=dtile, _s=slg: e.matmul(
                        PS[0][:, :], BIGB[:, 32768 + _s * 2048 + _d * 128: 32768 + _s * 2048 + _d * 128 + 128],
                        XLNB[:, _d * 512:(_d + 1) * 512],
                        start=(_d == 0), stop=(_d == DT - 1)), [("m", 1)] if dtile == DT - 1 else ())
                gate_m = P.val("m")
                for dtile in range(DT):
                    P.op("tensor", lambda e, _d=dtile, _s=slu: e.matmul(
                        PS[1][:, :], BIGB[:, 32768 + _s * 2048 + _d * 128: 32768 + _s * 2048 + _d * 128 + 128],
                        XLNB[:, _d * 512:(_d + 1) * 512],
                        start=(_d == 0), stop=(_d == DT - 1)), [("m", 1)] if dtile == DT - 1 else ())
                wg_ld[ot] = P.val("m")
                P.wait("scalar", "m", gate_m)
                P.op("scalar", lambda e, _o=ot: e.activation(
                    BIGB[:, _o * 512:(_o + 1) * 512], PS[0][:, :], AF.Silu,
                    scale=SGB[:, _o:_o + 1]), [("c", 1)])
                if ot == 0:
                    silu_dn = {}
                silu_dn[ot] = P.val("c")
                P.wait("vector", "m", wg_ld[ot])
                P.wait("vector", "c", silu_dn[ot])
                P.op("vector", lambda e, _o=ot: e.scalar_tensor_tensor(
                    BIGB[:, _o * 512:(_o + 1) * 512], PS[1][:, :], SUB[:, _o:_o + 1],
                    BIGB[:, _o * 512:(_o + 1) * 512], ALU.mult, ALU.mult), [("v", 1)])
                fuse_done[(fc, ot)] = P.val("v")
            dn_ev = {}
            for o2 in range(16):
                bank = PS[2 + o2 % 2]
                for hf in range(2):
                    P.wait("sync", "m")
                    dma("sync", f"wd{hf}", BIGB.ap().rearrange("p (n c) -> p n c", c=128)[:, 320 + hf * 32: 320 + hf * 32 + 32, :],
                        wdb.rearrange("o n p c -> o p n c")[o2][:, hf * 32:(hf + 1) * 32, :])
                    P.wait("tensor", f"wd{hf}")
                    if o2 == 0 and hf == 0:
                        P.wait("tensor", "v", fuse_done[(fc, 63)])
                    if o2 >= 2 and hf == 0:
                        P.wait("tensor", "v", dn_ev[o2 - 2])
                    for it2 in range(32):
                        itg = hf * 32 + it2
                        P.op("tensor", lambda e, _i=it2, _g=itg, _b=bank, _hf=hf, _last=(hf == 1 and it2 == 31): e.matmul(
                            _b[:, :], BIGB[:, 40960 + _hf * 4096 + _i * 128: 40960 + _hf * 4096 + _i * 128 + 128],
                            BIGB[:, _g * 512:(_g + 1) * 512],
                            start=(_g == 0), stop=_last), [("m", 1)] if (hf == 1 and it2 == 31) else ())
                stg = STG[:, (o2 % 2) * 512:(o2 % 2) * 512 + 512]
                slot_wait("vector", ("stg", o2 % 2))
                P.wait("vector", "m")
                P.op("vector", lambda e, _s=stg, _b=bank, _o=o2: e.scalar_tensor_tensor(
                    _s, _b[:, :], SDB[:, _o:_o + 1], F32B[:, _o * 512:(_o + 1) * 512],
                    ALU.mult, ALU.add), [("v", 1)])
                dn_ev[o2] = P.val("v")
                P.wait("gpsimd", "v")
                if app == 0:
                    dma("gpsimd", f"sg{o2 % 2}", x2c[fc][o2 * 128:(o2 + 1) * 128, :], stg)
                else:
                    dma("gpsimd", f"sg{o2 % 2}", out_x[o2 * 128:(o2 + 1) * 128, fc * 512:(fc + 1) * 512], stg)
                slot_set(("stg", o2 % 2), f"sg{o2 % 2}")
            if app == 0:
                P.wait("gpsimd", "sg0")
                P.wait("gpsimd", "sg1")
                P.op("gpsimd", lambda e, _c=fc: e.collective_compute(
                    "AllGather", ALU.bypass, replica_groups=PAIRS,
                    ins=[x2c[_c].opt()], outs=[agx2[_c].opt()]), [(f"cc{4 + fc}", 1)])

    ffn(0)
    if DBG:
        d_fuse = nc.dram_tensor("d_fuse", [128, 32768], BF16, kind="ExternalOutput").ap()
        d_xln1 = nc.dram_tensor("d_xln1", [128, 8192], BF16, kind="ExternalOutput").ap()
        P.wait("gpsimd", "v")
        P.wait("gpsimd", "c")
        dma("gpsimd", "st", d_fuse, BIGB[:, 0:32768])
        dma("gpsimd", "st", d_xln1, XLNB[:, 0:8192])
        for _e in ("sync", "vector", "scalar", "tensor"):
            wait_stores(_e)

    load_wres(A["wqlk"], 0, 3072)
    if DBG:
        d_wres = nc.dram_tensor("d_wres", [128, 49152], BF16, kind="ExternalOutput").ap()
        P.wait("gpsimd", "v")
        dma("gpsimd", "st", d_wres, BIGB[:, 0:49152])
        for _e in ("sync", "vector", "scalar", "tensor"):
            wait_stores(_e)
    for g in range(8):
        P.wait("sync", f"cc{4 + g % 4}", 1)
        P.wait("sync", "v")
        dma("sync", "xcs", F32B.ap().rearrange("p (n c) -> p n c", c=512)[:, 0:16, :],
            agx2[g % 4].rearrange("(n p) t -> p n t", p=128)[:, (g // 4) * 16:(g // 4) * 16 + 16, :])
        P.wait("scalar", "xcs")
        P.wait("tensor", "xcs")
        P.wait("vector", "xcs")
        ln_chunk()
        P.wait("sync", "v")
        dma("sync", "tbs", F32B[:, 1024:1536], A["ct"][:, g * 512:(g + 1) * 512])
        dma("sync", "tbs", F32B[:, 1536:2048], A["st_t"][:, g * 512:(g + 1) * 512])
        P.wait("vector", "tbs")
        P.wait("tensor", "v")
        for dtile in range(DT):
            P.op("tensor", lambda e, _d=dtile: e.matmul(
                PS[4][0:1, :], WCB[:, _d:_d + 1], XLNB[:, _d * 512:(_d + 1) * 512],
                start=(_d == 0), stop=(_d == DT - 1)), [("m", 1)] if dtile == DT - 1 else ())
        cwr = ROWS[0:1, 0:512]
        ecw = ROWS[0:1, 1024:1536]
        gsum = ROWS[0:1, 0:32]
        glog = ROWS[0:1, 32:64]
        ginv = ROWS[0:1, 64:96]
        P.wait("scalar", "m")
        P.op("scalar", lambda e: e.activation(cwr, PS[4][0:1, :], AF.Identity,
                                              bias=BCR[0:1, 0:1]), [("c", 1)])
        P.wait("scalar", "c")
        P.op("scalar", lambda e: e.activation(ecw, cwr, AF.Exp), [("c", 1)])
        P.wait("vector", "c")
        P.op("vector", lambda e: e.reduce_sum(
            gsum, ecw.rearrange("a (c r) -> a c r", r=16), AX.X), [("v", 1)])
        P.wait("scalar", "v")
        P.op("scalar", lambda e: e.activation(glog, gsum, AF.Ln), [("c", 1)])
        P.wait("scalar", "c")
        P.op("scalar", lambda e: e.activation(ginv, glog, AF.Exp, scale=-1.0), [("c", 1)])
        P.wait("tensor", "c")
        P.op("tensor", lambda e: e.matmul(PS[6][:, :], ONESR[0:1, :], ecw, start=True, stop=True))
        P.op("tensor", lambda e: e.matmul(PS[5][:, 0:32], ONESR[0:1, :], ginv, start=True, stop=True), [("m", 1)])
        P.wait("vector", "m")
        hw = F32B[:, 8192:8704]
        rawE = F32B[:, 8704:8736]
        for dtile in range(DT):
            P.op("vector", lambda e, _d=dtile: e.tensor_tensor(
                hw, XLNB[:, _d * 512:(_d + 1) * 512], PS[6][:, :], ALU.mult), [("v", 1)])
            P.wait("vector", "v")
            P.op("vector", lambda e: e.reduce_sum(
                rawE, hw.rearrange("p (c r) -> p c r", r=16), AX.X), [("v", 1)])
            P.wait("vector", "v")
            P.op("vector", lambda e, _d=dtile, _g=g: e.tensor_tensor(
                ENTR[:, _d * 256 + _g * 32: _d * 256 + _g * 32 + 32],
                rawE, PS[5][:, 0:32], ALU.mult), [("v", 1)])
            P.wait("vector", "v")
        qb = F32B[:, 0:512]
        sqr = F32B[:, 512:1024]
        QNSL = 12288
        USL = 12800
        VSL = 13312
        OSL = 13824
        for ot in range(16):
            bank = PS[ot % 2]
            if ot >= 2:
                P.wait("tensor", "c", qlk_ev[ot - 2])
            proj16(0, 3072, ot, bank)
            P.wait("scalar", "m")
            P.wait("scalar", "v")
            P.op("scalar", lambda e, _o=ot, _b=bank, _q=qb: e.activation(
                _q, _b[:, :], AF.Identity, bias=BQLK[:, _o:_o + 1]), [("c", 1)])
            if ot == 0:
                qlk_ev = {}
            qlk_ev[ot] = P.val("c")
            P.wait("scalar", "c")
            P.op("scalar", lambda e, _q=qb, _s=sqr: e.activation(_s, _q, AF.Square), [("c", 1)])
            P.wait("tensor", "c")
            P.op("tensor", lambda e, _s=sqr: e.matmul(PS[4][0:1, :], ONESC[:, 0:1], _s,
                                              start=True, stop=True), [("m", 1)])
            lnv2 = ROWS[0:1, 0:512]
            rr2 = ROWS[0:1, 512:1024]
            P.wait("scalar", "m")
            P.op("scalar", lambda e, _l=lnv2: e.activation(_l, PS[4][0:1, :], AF.Ln,
                                                           scale=1.0 / HD, bias=EPSR[0:1, 1:2]), [("c", 1)])
            P.wait("scalar", "c")
            P.op("scalar", lambda e, _l=lnv2, _r=rr2: e.activation(_r, _l, AF.Exp, scale=-0.5), [("c", 1)])
            P.wait("tensor", "c")
            P.op("tensor", lambda e, _r=rr2: e.matmul(PS[7][:, :], ONESR[0:1, :], _r,
                                                      start=True, stop=True), [("m", 1)])
            wvec = QNW if ot < 8 else KNW
            P.wait("vector", "m")
            P.op("vector", lambda e, _w=wvec, _q=qb: e.scalar_tensor_tensor(
                XLNB[:, QNSL:QNSL + 512], _q, _w[:, 0:1], PS[7][:, :],
                ALU.mult, ALU.mult), [("v", 1)])
            P.wait("tensor", "v")
            P.op("tensor", lambda e: e.matmul(PS[6][:, :], MPERM[:, :],
                                              XLNB[:, QNSL:QNSL + 512], start=True, stop=True), [("m", 1)])
            P.wait("vector", "m")
            P.op("vector", lambda e: e.tensor_tensor(
                XLNB[:, USL:USL + 512], XLNB[:, QNSL:QNSL + 512], F32B[:, 1024:1536], ALU.mult), [("v", 1)])
            P.op("vector", lambda e: e.tensor_tensor(
                XLNB[:, VSL:VSL + 512], PS[6][:, :], F32B[:, 1536:2048], ALU.mult), [("v", 1)])
            P.wait("vector", "v")
            ost = OSL + (ot % 2) * 512
            slot_wait("vector", ("rope", ot % 2))
            P.op("vector", lambda e, _o=ost: e.tensor_tensor(
                XLNB[:, _o:_o + 512], XLNB[:, USL:USL + 512], XLNB[:, VSL:VSL + 512],
                ALU.add), [("v", 1)])
            P.wait("gpsimd", "v")
            dst = sQbT if ot < 8 else sLkT
            dma("gpsimd", f"sr{ot % 2}", dst.rearrange("(n p) t -> n p t", p=128)[ot % 8][:, g * 512:(g + 1) * 512],
                XLNB[:, ost:ost + 512])
            slot_set(("rope", ot % 2), f"sr{ot % 2}")
        lv_ev = {}
        for vt in range(8):
            tt, nb = vt // 2, vt % 2
            bank = PS[2 + vt % 2]
            if vt >= 2:
                P.wait("tensor", "c", lv_ev[vt - 2])
            for dtile in range(DT):
                P.op("tensor", lambda e, _d=dtile, _t=tt, _n=nb, _b=bank: e.matmul(
                    _b[:, :],
                    XLNB[:, _d * 512 + _t * 128: _d * 512 + _t * 128 + 128],
                    BIGB[:, _d * 3072 + 2048 + _n * 512: _d * 3072 + 2048 + _n * 512 + 512],
                    start=(_d == 0), stop=False))
            P.op("tensor", lambda e, _n=nb, _b=bank: e.matmul(
                _b[:, :], ONESR[0:1, :], BLV[0:1, _n * 512:(_n + 1) * 512],
                start=False, stop=True), [("m", 1)])
            stg = XLNB[:, 8192 + (vt % 4) * 512: 8192 + (vt % 4) * 512 + 512]
            slot_wait("scalar", ("qkstg", vt % 4))
            P.wait("scalar", "m")
            P.op("scalar", lambda e, _b=bank, _s=stg: e.activation(_s, _b[:, :], AF.Copy), [("c", 1)])
            lv_ev[vt] = P.val("c")
            P.wait("gpsimd", "c")
            dma("gpsimd", f"sq{vt % 4}", sLv[g * 512 + tt * 128: g * 512 + tt * 128 + 128,
                                    nb * 512:(nb + 1) * 512], stg)
            slot_set(("qkstg", vt % 4), f"sq{vt % 4}")

    if DBG:
        d_xln = nc.dram_tensor("d_xln", [128, 8192], BF16, kind="ExternalOutput").ap()
        d_entr = nc.dram_tensor("d_entr", [128, 4096], BF16, kind="ExternalOutput").ap()
        P.wait("gpsimd", "v")
        dma("gpsimd", "st", d_xln, XLNB[:, 0:8192])
        dma("gpsimd", "st", d_entr, ENTR[:, :])
        for _e in ("sync", "vector", "scalar", "tensor"):
            wait_stores(_e)
    dma("sync", "wkv", F32B[:, 0:2048], A["wk"])
    dma("sync", "wkv", F32B[:, 2048:4096], A["wv"])
    P.wait("vector", "wkv")
    wait_stores("vector")
    P.wait("vector", "m")
    P.op("vector", lambda e: e.tensor_copy(XLNB[:, 0:2048], F32B[:, 0:2048]), [("v", 1)])
    P.op("vector", lambda e: e.tensor_copy(XLNB[:, 2048:4096], F32B[:, 2048:4096]), [("v", 1)])
    P.wait("tensor", "v")
    for ct in range(2):
        P.wait("tensor", "c")
        P.wait("tensor", "v")
        for dtile in range(DT):
            P.op("tensor", lambda e, _d=dtile, _c=ct: e.matmul(
                PS[0][:, 0:128], XLNB[:, _d * 128:(_d + 1) * 128],
                ENTR[:, _d * 256 + _c * 128: _d * 256 + _c * 128 + 128],
                start=(_d == 0), stop=(_d == DT - 1)), [("m", 1)] if dtile == DT - 1 else ())
        qb = F32B[:, 0:128]
        sqr = F32B[:, 128:256]
        P.wait("scalar", "m")
        P.wait("scalar", "v")
        P.op("scalar", lambda e, _q=qb: e.activation(_q, PS[0][:, 0:128], AF.Identity,
                                                     bias=BKR[:, 0:1]), [("c", 1)])
        P.wait("scalar", "c")
        P.op("scalar", lambda e, _q=qb, _s=sqr: e.activation(_s, _q, AF.Square), [("c", 1)])
        P.wait("tensor", "c")
        P.op("tensor", lambda e, _s=sqr: e.matmul(PS[4][0:1, 0:128], ONESC[:, 0:1], _s,
                                                  start=True, stop=True), [("m", 1)])
        l2 = ROWS[0:1, 0:128]
        r2 = ROWS[0:1, 512:640]
        P.wait("scalar", "m")
        P.op("scalar", lambda e, _l=l2: e.activation(_l, PS[4][0:1, 0:128], AF.Ln,
                                                     scale=1.0 / HD, bias=EPSR[0:1, 1:2]), [("c", 1)])
        P.wait("scalar", "c")
        P.op("scalar", lambda e, _l=l2, _r=r2: e.activation(_r, _l, AF.Exp, scale=-0.5), [("c", 1)])
        P.wait("tensor", "c")
        P.op("tensor", lambda e, _r=r2: e.matmul(PS[7][:, 0:128], ONESR[0:1, :], _r,
                                                 start=True, stop=True), [("m", 1)])
        P.wait("vector", "m")
        P.op("vector", lambda e, _q=qb: e.scalar_tensor_tensor(
            XLNB[:, 4096:4224], _q, KNW[:, 0:1], PS[7][:, 0:128], ALU.mult, ALU.mult), [("v", 1)])
        P.wait("tensor", "v")
        P.op("tensor", lambda e: e.matmul(PS[6][:, 0:128], MPERM[:, :], XLNB[:, 4096:4224],
                                          start=True, stop=True), [("m", 1)])
        P.wait("vector", "m")
        P.op("vector", lambda e, _c=ct: e.tensor_tensor(
            XLNB[:, 4224:4352], XLNB[:, 4096:4224], CKTAB[:, _c * 128:(_c + 1) * 128], ALU.mult), [("v", 1)])
        P.op("vector", lambda e, _c=ct: e.tensor_tensor(
            XLNB[:, 4352:4480], PS[6][:, 0:128], SKTAB[:, _c * 128:(_c + 1) * 128], ALU.mult), [("v", 1)])
        P.wait("vector", "v")
        P.op("vector", lambda e, _c=ct: e.tensor_tensor(
            CKT[:, _c * 128:(_c + 1) * 128], XLNB[:, 4224:4352], XLNB[:, 4352:4480], ALU.add), [("v", 1)])
        P.wait("tensor", "v")
        for dtile in range(DT):
            P.op("tensor", lambda e, _d=dtile, _c=ct: e.matmul(
                PS[1][:, 0:128], ENTR[:, _d * 256 + _c * 128: _d * 256 + _c * 128 + 128],
                XLNB[:, 2048 + _d * 128: 2048 + (_d + 1) * 128],
                start=(_d == 0), stop=False))
        P.op("tensor", lambda e: e.matmul(PS[1][:, 0:128], ONESR[0:1, :], BVR[0:1, :],
                                          start=False, stop=True), [("m", 1)])
        P.wait("scalar", "m")
        P.op("scalar", lambda e, _c=ct: e.activation(CVT[:, _c * 128:(_c + 1) * 128],
                                                     PS[1][:, 0:128], AF.Copy), [("c", 1)])

    wo_off = 26624
    P.wait("sync", "m")
    P.wait("sync", "c")
    P.wait("vector", "m")
    for k in range(4):
        P.wait("sync", "v")
        dma("sync", "wps", F32B[:, 0:4096], A["wo"][:, k * 4096:(k + 1) * 4096])
        P.wait("vector", "wps")
        P.wait("vector", "c")
        for u in range(4):
            P.op("vector", lambda e, _k=k, _u=u: e.tensor_copy(
                BIGB[:, wo_off + _k * 4096 + _u * 1024: wo_off + _k * 4096 + (_u + 1) * 1024],
                F32B[:, _u * 1024:(_u + 1) * 1024]), [("v", 1)])

    LKH = [0, 1280]
    LVH = [2560, 3840]
    QH2 = [5120, 5632]

    def flash_b():
        wait_stores("sync")
        P.wait("sync", "m")
        P.wait("tensor", "v")
        P.wait("tensor", "c")
        it = -1
        last_mm = {}
        ctx_tt = {}
        den_rd = {}
        proj_done = {}
        qbT = sQbT.rearrange("(n p) t -> n p t", p=128)
        lkT = sLkT.rearrange("(n p) t -> n p t", p=128)
        for qi, qc in enumerate(QC_ORDER):
            bct = qc // 4
            kbase = 4 * qc - 1 if qc > 0 else 0
            nlk = 5 if qc > 0 else 4
            for h in range(8):
                it += 1
                sl = it % 2
                if h == 0 and qi > 0:
                    P.wait("tensor", "v")
                if it >= 2:
                    P.wait("sync", "m", last_mm[it - 2])
                fsem = f"fl{sl}"
                dma("sync", fsem, BIGB[:, LKH[sl]: LKH[sl] + nlk * 128],
                    lkT[h][:, kbase * 128: kbase * 128 + nlk * 128])
                dma("sync", fsem,
                    BIGB.ap().rearrange("p (n c) -> p n c", c=128)[:, LVH[sl] // 128: LVH[sl] // 128 + nlk, :],
                    sLv.rearrange("(n p) c -> p n c", p=128)[:, kbase:kbase + nlk, h * 128:(h + 1) * 128])
                dma("sync", fsem, BIGB[:, QH2[sl]: QH2[sl] + 512],
                    qbT[h][:, qc * 512:(qc + 1) * 512])
                ld_tgt = P.val(fsem)
                PC, PL0, AVC, MISC, PL1, AVL, RBP = PS[0], PS[1], PS[2], PS[3], PS[5], PS[6], PS[7]
                P.wait("tensor", fsem, ld_tgt)
                if it >= 1:
                    P.wait("tensor", "v", ctx_tt.get(it - 1, 0))
                    P.wait("tensor", "c", den_rd.get(it - 1, 0))
                exp_slots = {}
                for j in range(4):
                    qt = 4 * qc + j
                    if qt > 0:
                        P.op("tensor", lambda e, _j=j, _sl=sl, _i=(4 * qc + j - 1 - kbase): e.matmul(
                            PL0[:, _j * 128:(_j + 1) * 128],
                            BIGB[:, LKH[_sl] + _i * 128: LKH[_sl] + _i * 128 + 128],
                            BIGB[:, QH2[_sl] + _j * 128: QH2[_sl] + _j * 128 + 128],
                            start=True, stop=False))
                        P.op("tensor", lambda e, _j=j: e.matmul(
                            PL0[:, _j * 128:(_j + 1) * 128], TRIL[:, :], IDENT[:, :],
                            start=False, stop=True))
                    P.op("tensor", lambda e, _j=j, _sl=sl, _i=(4 * qc + j - kbase): e.matmul(
                        PL1[:, _j * 128:(_j + 1) * 128],
                        BIGB[:, LKH[_sl] + _i * 128: LKH[_sl] + _i * 128 + 128],
                        BIGB[:, QH2[_sl] + _j * 128: QH2[_sl] + _j * 128 + 128],
                        start=True, stop=False))
                    P.op("tensor", lambda e, _j=j: e.matmul(
                        PL1[:, _j * 128:(_j + 1) * 128], TRIU[:, :], IDENT[:, :],
                        start=False, stop=True), [("m", 1)] if j == 3 else ())
                loc_m = P.val("m")
                e0 = EXPO + (sl * 4) * 512
                e1 = EXPO + (sl * 4 + 1) * 512
                l0c0 = 128 if qc == 0 else 0
                P.wait("scalar", "m", loc_m)
                P.op("scalar", lambda e, _e=e0, _c0=l0c0: e.activation(
                    BIGB[:, _e + _c0: _e + 512], PL0[:, _c0:512], AF.Exp, scale=SCALE), [("c", 1)])
                P.op("scalar", lambda e, _e=e1: e.activation(
                    BIGB[:, _e: _e + 512], PL1[:, :], AF.Exp, scale=SCALE), [("c", 1)])
                exp_l = P.val("c")
                P.wait("tensor", "c", exp_l)
                P.op("tensor", lambda e, _e=e1: e.matmul(
                    MISC[32:33, :], ONESCB[:, 0:1], BIGB[:, _e: _e + 512],
                    start=True, stop=False))
                P.op("tensor", lambda e, _e=e0, _c0=l0c0: e.matmul(
                    MISC[32:33, _c0:512], ONESCB[:, 0:1], BIGB[:, _e + _c0: _e + 512],
                    start=False, stop=True))
                for j in range(4):
                    qt = 4 * qc + j
                    P.op("tensor", lambda e, _j=j, _sl=sl, _e=e1, _i=(4 * qc + j - kbase), _z=(qc == 0): e.matmul(
                        AVL[:, _j * 128:(_j + 1) * 128],
                        BIGB[:, LVH[_sl] + _i * 128: LVH[_sl] + _i * 128 + 128],
                        BIGB[:, _e + _j * 128: _e + _j * 128 + 128],
                        start=True, stop=(_j == 0 and _z)))
                    if qt > 0:
                        P.op("tensor", lambda e, _j=j, _sl=sl, _e=e0, _i=(4 * qc + j - 1 - kbase): e.matmul(
                            AVL[:, _j * 128:(_j + 1) * 128],
                            BIGB[:, LVH[_sl] + _i * 128: LVH[_sl] + _i * 128 + 128],
                            BIGB[:, _e + _j * 128: _e + _j * 128 + 128],
                            start=False, stop=True), [("m", 1)] if j == 3 else ())
                    if j == 3 and qt == 0:
                        pass
                ects = []
                for ct2 in range(bct + 1):
                    boundary = (ct2 == bct)
                    P.op("tensor", lambda e, _c=ct2, _sl=sl, _b=boundary: e.matmul(
                        PC[:, :], CKT[:, _c * 128:(_c + 1) * 128],
                        BIGB[:, QH2[_sl]: QH2[_sl] + 512],
                        start=True, stop=(not _b)),
                        [] if boundary else [("m", 1)])
                    if boundary:
                        for j in range(4):
                            o = (4 * qc + j) % 16
                            P.op("tensor", lambda e, _j=j, _o=o, _l=(j == 3): e.matmul(
                                PC[:, _j * 128:(_j + 1) * 128], STAIR[:, _o * 128:(_o + 1) * 128],
                                IDENT[:, :], start=False, stop=_l),
                                [("m", 1)] if j == 3 else ())
                    pcm = P.val("m")
                    ec = EXPO + (sl * 4 + 2 + ct2) * 512
                    ects.append(ec)
                    P.wait("scalar", "m", pcm)
                    P.op("scalar", lambda e, _e=ec: e.activation(
                        BIGB[:, _e: _e + 512], PC[:, :], AF.Exp, scale=SCALE), [("c", 1)])
                    P.wait("tensor", "c")
                P.op("tensor", lambda e, _h=h, _sl=sl: e.matmul(
                    MISC[64:65, :], SINKK[:, _h:_h + 1], BIGB[:, QH2[_sl]: QH2[_sl] + 512],
                    start=True, stop=True), [("m", 1)])
                P.wait("scalar", "m")
                P.op("scalar", lambda e: e.activation(XLNB[0:1, 4608:5120], MISC[64:65, :],
                                                      AF.Exp, scale=SCALE), [("c", 1)])
                sink_c = P.val("c")
                P.wait("tensor", "c", sink_c)
                for ci, ec in enumerate(ects):
                    P.op("tensor", lambda e, _e=ec, _f=(ci == 0): e.matmul(
                        MISC[0:1, :], ONESCB[:, 0:1], BIGB[:, _e: _e + 512],
                        start=_f, stop=False))
                P.op("tensor", lambda e: e.matmul(
                    MISC[0:1, :], ONESCB[0:1, 0:1], XLNB[0:1, 4608:5120],
                    start=False, stop=True))
                for ci, ec in enumerate(ects):
                    P.op("tensor", lambda e, _e=ec, _c=ci, _f=(ci == 0): e.matmul(
                        AVC[:, :], CVT[:, _c * 128:(_c + 1) * 128], BIGB[:, _e: _e + 512],
                        start=_f, stop=False))
                P.op("tensor", lambda e, _h=h: e.matmul(
                    AVC[:, :], SINKV[0:1, _h * 128:(_h + 1) * 128], XLNB[0:1, 4608:5120],
                    start=False, stop=True), [("m", 1)])
                last_mm[it] = P.val("m")
                dc = ROWS[0:1, 0:512]
                dl = ROWS[0:1, 512:1024]
                t0 = ROWS[0:1, 1024:1536]
                P.wait("scalar", "m", last_mm[it])
                P.op("scalar", lambda e: e.activation(dc, MISC[0:1, :], AF.Copy), [("c", 1)])
                P.op("scalar", lambda e: e.activation(dl, MISC[32:33, :], AF.Copy), [("c", 1)])
                P.wait("scalar", "c")
                P.op("scalar", lambda e: e.activation(t0, dc, AF.Ln), [("c", 1)])
                P.wait("scalar", "c")
                P.op("scalar", lambda e: e.activation(dc, t0, AF.Exp, scale=-1.0), [("c", 1)])
                P.wait("scalar", "c")
                P.op("scalar", lambda e: e.activation(t0, dl, AF.Ln), [("c", 1)])
                P.wait("scalar", "c")
                P.op("scalar", lambda e: e.activation(dl, t0, AF.Exp, scale=-1.0), [("c", 1)])
                den_rd[it] = P.val("c")
                P.wait("tensor", "c", den_rd[it])
                P.op("tensor", lambda e: e.matmul(RBP[:, :], HALFR[0:1, :], dc,
                                                  start=True, stop=True), [("m", 1)])
                rbc = RBB + 0
                rbl = RBB + 512
                P.wait("vector", "m")
                if qi > 0:
                    P.wait("vector", "m", proj_done.get(qi - 1, 0))
                P.op("vector", lambda e, _r=rbc: e.tensor_copy(BIGB[:, _r:_r + 512], RBP[:, :]), [("v", 1)])
                rbc_v = P.val("v")
                P.wait("tensor", "v", rbc_v)
                P.op("tensor", lambda e: e.matmul(RBP[:, :], HALFR[0:1, :], dl,
                                                  start=True, stop=True), [("m", 1)])
                P.wait("vector", "m")
                P.op("vector", lambda e, _r=rbl: e.tensor_copy(BIGB[:, _r:_r + 512], RBP[:, :]), [("v", 1)])
                P.wait("vector", "v")
                P.op("vector", lambda e, _r=rbc: e.tensor_tensor(
                    XLNB[:, 5120:5632], AVC[:, :], BIGB[:, _r:_r + 512], ALU.mult), [("v", 1)])
                P.wait("vector", "v")
                P.op("vector", lambda e, _r=rbl: e.tensor_tensor(
                    XLNB[:, 5632:6144], AVL[:, :], BIGB[:, _r:_r + 512], ALU.mult), [("v", 1)])
                P.wait("vector", "v")
                P.op("vector", lambda e, _h=h: e.tensor_tensor(
                    BIGB[:, CTXQ + _h * 512: CTXQ + _h * 512 + 512],
                    XLNB[:, 5120:5632], XLNB[:, 5632:6144], ALU.add), [("v", 1)])
                ctx_tt[it] = P.val("v")
            if DBG and qc == 0:
                d_ctxb = nc.dram_tensor("d_ctxb", [128, 4096], BF16, kind="ExternalOutput").ap()
                d_ckcv = nc.dram_tensor("d_ckcv", [128, 512], BF16, kind="ExternalOutput").ap()
                P.wait("gpsimd", "v", ctx_tt[it])
                dma("gpsimd", "st", d_ctxb, BIGB[:, CTXQ:CTXQ + 4096])
                dma("gpsimd", "st", d_ckcv[:, 0:256], CKT[:, :])
                dma("gpsimd", "st", d_ckcv[:, 256:512], CVT[:, :])
                wait_stores("vector")
                wait_stores("tensor")
                wait_stores("scalar")
                wait_stores("sync")
            P.wait("tensor", "v", ctx_tt[it])
            pj_ev = {}
            pj_st = {}
            for o2 in range(16):
                bank = PS[5 + o2 % 2]
                if o2 >= 2:
                    P.wait("tensor", "v", pj_ev[o2 - 2])
                for dh in range(8):
                    P.op("tensor", lambda e, _d=dh, _o=o2, _b=bank: e.matmul(
                        _b[:, :],
                        BIGB[:, wo_off + _d * 2048 + _o * 128: wo_off + _d * 2048 + _o * 128 + 128],
                        BIGB[:, CTXQ + _d * 512: CTXQ + _d * 512 + 512],
                        start=(_d == 0), stop=(_d == 7)),
                        [("m", 1)] if dh == 7 else ())
                stg = STG[:, (o2 % 2) * 512:(o2 % 2) * 512 + 512]
                slot_wait("vector", ("stg", o2 % 2))
                P.wait("vector", "m")
                P.op("vector", lambda e, _s=stg, _b=bank: e.tensor_copy(_s, _b[:, :]), [("v", 1)])
                pj_ev[o2] = P.val("v")
                P.wait("gpsimd", "v")
                dma("gpsimd", f"sg{o2 % 2}", rs2in[qc % 4][(qc // 4) * 2048 + o2 * 128:
                                                   (qc // 4) * 2048 + o2 * 128 + 128, :], stg)
                slot_set(("stg", o2 % 2), f"sg{o2 % 2}")
            proj_done[qi] = P.val("m")
            if qc >= 4:
                cci = qc - 4
                P.wait("gpsimd", "sg0")
                P.wait("gpsimd", "sg1")
                P.op("gpsimd", lambda e, _c=cci: e.collective_compute(
                    "ReduceScatter", ALU.add, replica_groups=PAIRS,
                    ins=[rs2in[_c].opt()], outs=[rs2out[_c].opt()]), [(f"cc{8 + cci}", 1)])

    flash_b()

    ffn(1)
    if DBG:
        dbg_list = [("d_sQT", sQT, [1024, T], BF16), ("d_sKT", sKT, [1024, T], BF16),
                    ("d_sV", sV, [T, 1024], BF16),
                    ("d_rs1in0", rs1in[0], [2 * D, 512], F32),
                    ("d_rs1out0", rs1out[0], [D, 512], F32),
                    ("d_x2c0", x2c[0], [D, 512], F32),
                    ("d_agx20", agx2[0], [2 * D, 512], F32),
                    ("d_sQbT", sQbT, [1024, T], BF16), ("d_sLkT", sLkT, [1024, T], BF16),
                    ("d_sLv", sLv, [T, 1024], BF16),
                    ("d_rs2out0", rs2out[0], [D, 512], F32),
                    ("d_wgb", wgb, [64 * DT * 128, 128], BF16)]
        wait_stores("gpsimd")
        for nm, srcap, sh, dt in dbg_list:
            dsg = nc.dram_tensor(nm, list(sh), dt, kind="ExternalOutput").ap()
            if nm == "d_wgb":
                dma("gpsimd", "st", dsg, srcap.rearrange("a b c d -> (a b c) d"))
            else:
                dma("gpsimd", "st", dsg, srcap)
    wait_stores("gpsimd")

    with nc.Block() as block:
        @block.sync
        def _(e):
            for f in P.items["sync"]:
                f(e)

        @block.tensor
        def _(e):
            for f in P.items["tensor"]:
                f(e)

        @block.vector
        def _(e):
            for f in P.items["vector"]:
                f(e)

        @block.scalar
        def _(e):
            for f in P.items["scalar"]:
                f(e)

        @block.gpsimd
        def _(e):
            for f in P.items["gpsimd"]:
                f(e)

    ctx.close()
    return nc


def kernel(**inputs):
    C = host_constants()
    x0 = np.asarray(inputs["L0_attn_a_encodings"], dtype=np.float32)
    w_attn = _f(inputs["L0_attn_a_w_attn"])
    b_attn = _f(inputs["L0_attn_a_b_attn"])
    w_proj = _f(inputs["L0_attn_a_w_proj"])
    b_proj = _f(inputs["L0_attn_a_b_proj"])
    Wq = _f(inputs["L0_attn_b_Wq"]); bq = _f(inputs["L0_attn_b_bq"])
    Wc = _f(inputs["L0_attn_b_Wc"])
    Wk = _f(inputs["L0_attn_b_Wk"]); bk = _f(inputs["L0_attn_b_bk"])
    Wv = _f(inputs["L0_attn_b_Wv"]); bv = _f(inputs["L0_attn_b_bv"])
    Wlk = _f(inputs["L0_attn_b_Wlk"]); blk = _f(inputs["L0_attn_b_blk"])
    Wlv = _f(inputs["L0_attn_b_Wlv"]); blv = _f(inputs["L0_attn_b_blv"])
    qn_w = _f(inputs["L0_attn_b_qn_w"]); kn_w = _f(inputs["L0_attn_b_kn_w"])
    sink_k = _f(inputs["L0_attn_b_sink_k"]); sink_v = _f(inputs["L0_attn_b_sink_v"])
    Wo = _f(inputs["L0_attn_b_Wo"]); bo = _f(inputs["L0_attn_b_bo"])
    wg8 = np.ascontiguousarray(np.asarray(inputs["L0_ffn_w_gate"], dtype=np.int8))
    wu8 = np.ascontiguousarray(np.asarray(inputs["L0_ffn_w_up"], dtype=np.int8))
    wd8 = np.ascontiguousarray(np.asarray(inputs["L0_ffn_w_down"], dtype=np.int8))
    s_gate = _f(inputs["L0_ffn_s_gate"]); s_up = _f(inputs["L0_ffn_s_up"])
    s_down = _f(inputs["L0_ffn_s_down"])

    nc = build_nc()
    in_maps = []
    for core in range(8):
        b, hh = core // 2, core % 2
        hs = slice(hh * 1024, (hh + 1) * 1024)
        m = dict(C)
        x0T = np.ascontiguousarray(x0[b].T)
        m["x0T"] = x0T
        m["x0ownT"] = np.ascontiguousarray(x0T[:, hh * 2048:(hh + 1) * 2048])
        m["wqkv"] = np.ascontiguousarray(np.concatenate(
            [w_attn[:, hs], w_attn[:, 2048 + hh * 1024: 2048 + (hh + 1) * 1024],
             w_attn[:, 4096 + hh * 1024: 4096 + (hh + 1) * 1024]], axis=1))
        bq_sl = np.concatenate([b_attn[hs], b_attn[2048 + hh * 1024: 2048 + (hh + 1) * 1024]])
        m["bqkv"] = _f(bq_sl.reshape(16, 128).T)
        m["vbias"] = _f(b_attn[4096 + hh * 1024: 4096 + (hh + 1) * 1024].reshape(1, 1024))
        m["wproj"] = _f(tile_rows(w_proj[hs, :]))
        m["bproj"] = _f(b_proj.reshape(16, 128).T)
        m["wqlk"] = np.ascontiguousarray(np.concatenate(
            [Wq[:, hs], Wlk[:, hs], Wlv[:, hs]], axis=1))
        m["bqlk"] = _f(np.concatenate([bq[hs], blk[hs]]).reshape(16, 128).T)
        m["blv"] = _f(blv[hs].reshape(1, 1024))
        m["wo"] = _f(tile_rows(Wo[hs, :]))
        m["bo"] = _f(bo.reshape(16, 128).T)
        m["wc"] = _f(Wc.reshape(16, 128).T)
        m["wk"] = _f(tile_rows(Wk))
        m["wv"] = _f(tile_rows(Wv))
        m["bk"] = _f(bk.reshape(128, 1))
        m["bv"] = _f(bv.reshape(1, 128))
        m["qnw"] = _f(qn_w.reshape(128, 1))
        m["knw"] = _f(kn_w.reshape(128, 1))
        m["sinkk"] = _f(sink_k[hh * 8:(hh + 1) * 8].T)
        m["sinkv"] = _f(sink_v[hh * 8:(hh + 1) * 8].reshape(1, 1024))
        m["wg8"] = wg8
        m["wu8"] = wu8
        m["wd8"] = wd8
        m["sg"] = _f(s_gate.reshape(64, 128).T)
        m["su"] = _f(s_up.reshape(64, 128).T)
        m["sd"] = _f(s_down.reshape(16, 128).T)
        m["epsr"] = _f(np.array([[LN_EPS, RMS_EPS]], dtype=np.float32))
        m["bcr"] = _f(np.asarray(inputs["L0_attn_b_bc"], dtype=np.float32).reshape(1, 1))
        in_maps.append(m)

    res = run_bass_kernel_spmd(nc, in_maps, core_ids=list(range(8)))
    out = np.empty((4, 4096, 2048), dtype=np.float32)
    for core in range(8):
        b, hh = core // 2, core % 2
        out[b, hh * 2048:(hh + 1) * 2048, :] = np.asarray(res.results[core]["out_x"]).T
    return out
